# version 05 (final)
# baseline (speedup 1.0000x reference)
import os
import contextlib
import numpy as np
import concourse.bass as bass
import concourse.mybir as mybir
from concourse.bass import IndirectOffsetOnAxis
from concourse.bass_utils import run_bass_kernel_spmd

F32 = mybir.dt.float32
BF16 = mybir.dt.bfloat16
I32 = mybir.dt.int32
AF = mybir.ActivationFunctionType
ALU = mybir.AluOpType
AX = mybir.AxisListType

NCORE = 8
B, L, T, D = 4, 4096, 16384, 1024
N, RK, E, HID = 16, 64, 8, 4096
CAP = 2458
CAPP = 2560
TC, DC = 2048, 128
HF = 2048
NG = 16
NJ = 16
NF, FCH = 5, 512
XROWS = E * CAPP
TRASH = XROWS
LNEPS = 1e-5

DBG_STAGE = int(os.environ.get("MK_STAGE", "99"))
DBG_SUB = int(os.environ.get("MK_SUB", "99"))
DBG_R6 = int(os.environ.get("MK_R6", "99"))

ENGS = ("sync", "vector", "scalar", "tensor", "gpsimd")


PARANOID = int(os.environ.get("MK_PARANOID", "0"))
PFIX = set(x for x in os.environ.get("MK_PFIX", "").split(",") if x)


class Sched:
    def __init__(self):
        self.ops = []
        self.cnt = {e: 0 for e in ENGS}
        self.last_wait = {}

    def op(self, eng, fn, waits=(), inc=0, tag=None):
        for (s, v) in waits:
            assert v <= self.cnt[s], f"wait on future: {s} {v} > {self.cnt[s]}"
        if PARANOID == 1 or (PARANOID == 2 and eng == "vector") or \
           (PARANOID == 3 and eng == "scalar") or (PARANOID == 4 and eng == "tensor") or \
           (tag is not None and tag in PFIX) or tag in ("BARRIER", "dts"):
            waits = [(s, self.cnt[s]) for s in ENGS if s != eng]
        if inc:
            self.cnt[eng] += inc
        self.ops.append((eng, fn, [(s, v) for (s, v) in waits if v > 0], inc))

    def val(self, eng):
        return self.cnt[eng]

    def replay(self, eng_name, eng_obj, sems):
        for (eng, fn, waits, inc) in self.ops:
            if eng != eng_name:
                continue
            for (s, v) in waits:
                key = (eng_name, s)
                if self.last_wait.get(key, -1) >= v:
                    continue
                self.last_wait[key] = v
                eng_obj.wait_ge(sems[s], v)
            inst = fn(eng_obj)
            if inc:
                assert inst is not None
                inst.then_inc(sems[eng_name], inc)


def build_program():
    nc = bass.Bass("TRN2", target_bir_lowering=False, num_devices=NCORE)

    def din(name, shape, dtype=F32):
        return nc.dram_tensor(name, list(shape), dtype, kind="ExternalInput").ap()

    def dint(name, shape, dtype=F32):
        return nc.dram_tensor(name, list(shape), dtype, kind="Internal").ap()

    xf_d = din("xf", (DC, T))
    xo_d = din("xo", (D, TC))
    wx_d = din("wx", (D, 96))
    wdt_d = din("wdt", (RK, DC))
    bdt_d = din("bdt", (DC, 1))
    dsk_d = din("dsk", (DC, 1))
    aselL_d = din("aselL", (DC, NG * DC))
    vsel_d = din("vsel", (DC, NG * DC))
    bsel_d = din("bsel", (N, DC))
    rsel_d = din("rsel", (DC, NG * DC))
    tri_d = din("tri", (DC, DC))
    ones_d = din("onescol", (DC, 1))
    onesrow_d = din("onesrow", (1, DC))
    ident_d = din("ident", (DC, DC))
    iotaE_d = din("iotaE", (DC, DC))
    m9_d = din("m9", (DC, DC))
    rwT_d = din("rwT", (D, 8))
    rb_d = din("rb", (8, 1))
    maskC_d = din("maskC", (8, 1))
    w1T_d = din("w1T", (D, HID))
    w1b_d = din("w1b", (DC, 32))
    w2cm_d = din("w2cm", (8, DC, 32, DC))
    w2b_d = din("w2b", (DC, 8))
    lss_d = din("lss", (DC, 8))
    cvec_d = din("cvec", (DC, 8))
    yrows_d = din("yrows", (DC, 8), I32)
    xgrows_d = din("xgrows", (DC, 4), I32)

    ag1i_d = dint("ag1i", (98, TC))
    ag1o_d = dint("ag1o", (NCORE * 98, TC))
    ybin = [dint(f"ybin{b}", (DC, L)) for b in range(B)]
    yago_all = dint("yago_all", (B * NCORE * DC, L))
    yago = [yago_all[b * NCORE * DC:(b + 1) * NCORE * DC, :] for b in range(B)]
    cnt_i = dint("cnti", (1, 128))
    cnt_o = dint("cnto", (NCORE, 128))
    xl_d = dint("xl", (XROWS + 1, D), BF16)
    xs_d = [dint(f"xs{f}", (FCH, D), BF16) for f in range(NF)]
    xg_d = [dint(f"xg{f}", (8 * FCH, D), BF16) for f in range(NF)]
    zin_d = [dint(f"zin{f}", (FCH, D), BF16) for f in range(NF)]
    zr_d = dint("zr", (NF * NCORE * FCH, D), BF16)
    w1b16_d = dint("w1b16", (D, HID), BF16)
    w2b16_d = dint("w2b16", (8, DC, 32, DC), BF16)
    x1f_d = dint("x1f", (D, TC))
    ln3f_d = dint("ln3f", (D, TC))
    ln3tok_d = dint("ln3tok", (TC, D))
    x1tok_d = dint("x1tok", (TC, D))
    wgtd_d = dint("wgtd", (DC, 48))
    zrid_d = dint("zrid", (DC, 32), I32)
    barb_d = dint("barb", (1, 16384))

    out_d = nc.dram_tensor("out", [TC, D], F32, kind="ExternalOutput").ap()

    sems = {e: nc.alloc_semaphore(f"sem_{e}") for e in ENGS}
    S = Sched()
    root = contextlib.ExitStack()

    def sb(name, shape, dtype=F32, st=None):
        return (st or root).enter_context(nc.sbuf_tensor(name, list(shape), dtype)).ap()

    def ps(name, shape, dtype=F32, st=None):
        return (st or root).enter_context(nc.psum_tensor(name, list(shape), dtype)).ap()

    ones_sb = sb("ones_s", (DC, 1))
    onesrow_sb = sb("onesrow_s", (1, DC))
    ident_sb = sb("ident_s", (DC, DC))
    tri_sb = sb("tri_s", (DC, DC))
    iotaE_sb = sb("iotaE_s", (DC, DC))
    m9_sb = sb("m9_s", (DC, DC))
    rwT_sb = sb("rwT_s", (DC, 64))
    rb_sb = sb("rb_s", (8, 1))
    maskC_sb = sb("maskC_s", (8, 1))
    w1bias_sb = sb("w1bias_s", (DC, 32))
    w2b_sb = sb("w2b_s", (DC, 8))
    lss_sb = sb("lss_s", (DC, 8))
    lsb2_sb = sb("lsb2_s", (DC, 8))
    cvec_sb = sb("cvec_s", (DC, 8))
    CONSTS = [LNEPS, 1.0, -1.0, -0.5, 1.0 / D, 0.5, 2.0, 0.0]
    for ci, cval in enumerate(CONSTS):
        nc.const_aps.register(F32, cval, cvec_sb[:, ci:ci + 1])

    stM = contextlib.ExitStack()
    asel_sb = sb("asel_s", (DC, NG * DC), st=stM)
    vsel_sb = sb("vsel_s", (DC, NG * DC), st=stM)
    bsel_sb = sb("bsel_s", (N, DC), st=stM)
    rsel_sb = sb("rsel_s", (DC, NG * DC), st=stM)
    wdt_sb = sb("wdt_s", (RK, DC), st=stM)
    bdt_sb = sb("bdt_s", (DC, 1), st=stM)
    dsk_sb = sb("dsk_s", (DC, 1), st=stM)
    wx_sb = sb("wx_s", (DC, 8 * 96), st=stM)
    zsrc_sb = sb("zsrc", (DC, 2048), BF16, st=stM)
    cast_sb = [sb(f"cast{i}", (DC, 2048), BF16, st=stM) for i in range(2)]

    for (dst, src) in [
        (asel_sb, aselL_d), (vsel_sb, vsel_d), (bsel_sb, bsel_d),
        (rsel_sb, rsel_d), (tri_sb, tri_d), (ones_sb, ones_d),
        (onesrow_sb, onesrow_d), (ident_sb, ident_d), (iotaE_sb, iotaE_d),
        (m9_sb, m9_d), (rb_sb, rb_d), (maskC_sb, maskC_d), (wdt_sb, wdt_d),
        (bdt_sb, bdt_d), (dsk_sb, dsk_d), (w1bias_sb, w1b_d),
        (w2b_sb, w2b_d), (lss_sb, lss_d), (cvec_sb, cvec_d),
    ]:
        S.op("sync", lambda e, d=dst, s=src: e.dma_start(out=d[:], in_=s), inc=16)
    S.op("sync", lambda e: e.dma_start(
        out=rwT_sb[:].rearrange("p (kc e) -> p kc e", e=8),
        in_=rwT_d.rearrange("(kc p) e -> p kc e", p=DC)), inc=16)
    S.op("sync", lambda e: e.dma_start(
        out=wx_sb[:].rearrange("p (kc f) -> p kc f", f=96),
        in_=wx_d.rearrange("(kc p) f -> p kc f", p=DC)), inc=16)
    consts_done = S.val("sync")

    S.op("scalar", lambda e: e.activation(out=asel_sb[:], in_=asel_sb[:], func=AF.Exp),
         waits=[("sync", consts_done)], inc=1)
    asel_ready = S.val("scalar")
    S.op("vector", lambda e: e.tensor_tensor(lsb2_sb[:], lss_sb[:], w2b_sb[:], ALU.mult),
         waits=[("sync", consts_done)], inc=1)

    S.op("vector", lambda e: e.memset(zsrc_sb[:], 0.0), inc=1)
    zs_ready = S.val("vector")
    xl_flat = xl_d.rearrange("r c -> (r c)")
    CH = DC * 2048
    for i in range(80):
        S.op("gpsimd", lambda e, i=i: e.dma_start(
            out=xl_flat[i * CH:(i + 1) * CH], in_=zsrc_sb[:]),
            waits=[("vector", zs_ready)], inc=16)
    memset_done = S.val("gpsimd")

    w1T_r = w1T_d.rearrange("(kc p) (h m) -> kc h p m", p=DC, m=2048)
    w1b16_r = w1b16_d.rearrange("(kc p) (h m) -> kc h p m", p=DC, m=2048)
    w2f32 = w2cm_d.rearrange("a p b m -> a (b m) p", m=DC).rearrange("a (h m) p -> a h p m", m=2048)
    w2b16f = w2b16_d.rearrange("a p b m -> a (b m) p", m=DC).rearrange("a (h m) p -> a h p m", m=2048)
    cast_jobs = [(w1T_r[i // 2, i % 2], w1b16_r[i // 2, i % 2]) for i in range(16)]
    cast_jobs += [(w2f32[i // 2, i % 2], w2b16f[i // 2, i % 2]) for i in range(16)]
    prev_done = [0, 0]
    for i, (src, dst) in enumerate(cast_jobs):
        buf = cast_sb[i % 2]
        w = [("gpsimd", prev_done[i % 2])] if prev_done[i % 2] else []
        S.op("gpsimd", lambda e, b=buf, s=src: e.dma_start(out=b[:], in_=s), waits=w, inc=16)
        ld1 = S.val("gpsimd")
        S.op("gpsimd", lambda e, b=buf, d=dst: e.dma_start(out=d, in_=b[:]),
             waits=[("gpsimd", ld1)], inc=16)
        prev_done[i % 2] = S.val("gpsimd")
    cast_done = S.val("gpsimd")

    st0 = contextlib.ExitStack()
    xkc = sb("xkc", (DC, TC), st=st0)
    skc = sb("skc", (DC, TC), st=st0)
    mrow = sb("mrow", (1, TC), st=st0)
    s2row = sb("s2row", (1, TC), st=st0)
    rrow = sb("rrow", (1, TC), st=st0)
    xp_sb = sb("xp0", (96, TC), st=st0)
    pA = ps("pA", (DC, TC), st=st0)
    pB = ps("pB", (DC, TC), st=st0)

    xo_r = xo_d.rearrange("(kc p) t -> kc p t", p=DC)
    xkc_free = ("sync", 0)
    skc_cons = ("sync", 0)
    for kc in range(8):
        S.op("sync", lambda e, kc=kc: e.dma_start(out=xkc[:], in_=xo_r[kc]),
             waits=[xkc_free, ("sync", consts_done)], inc=16)
        ld = S.val("sync")
        S.op("vector", lambda e: e.tensor_tensor(skc[:], xkc[:], xkc[:], ALU.mult),
             waits=[("sync", ld)] + ([skc_cons] if skc_cons[1] else []), inc=1)
        sq_done = S.val("vector")
        for f in range(4):
            S.op("tensor", lambda e, f=f, kc=kc: e.matmul(
                pA[0:1, f * 512:(f + 1) * 512], ones_sb[:],
                xkc[:, f * 512:(f + 1) * 512],
                start=(kc == 0), stop=(kc == 7)),
                waits=[("sync", ld)])
        for f in range(4):
            S.op("tensor", lambda e, f=f, kc=kc: e.matmul(
                pB[0:1, f * 512:(f + 1) * 512], ones_sb[:],
                skc[:, f * 512:(f + 1) * 512],
                start=(kc == 0), stop=(kc == 7)),
                waits=[("vector", sq_done)],
                inc=(1 if f == 3 else 0))
        xkc_free = ("tensor", S.val("tensor"))
        skc_cons = ("tensor", S.val("tensor"))
    stats_mm = S.val("tensor")
    S.op("scalar", lambda e: e.activation(out=mrow[:], in_=pA[0:1, :], func=AF.Copy,
                                          scale=1.0 / D),
         waits=[("tensor", stats_mm)], inc=1)
    S.op("scalar", lambda e: e.activation(out=s2row[:], in_=pB[0:1, :], func=AF.Copy,
                                          scale=1.0 / D), inc=1)
    stats_evac = S.val("scalar")
    S.op("vector", lambda e: e.tensor_tensor(rrow[:], mrow[:], mrow[:], ALU.mult),
         waits=[("scalar", stats_evac)], inc=1)
    S.op("vector", lambda e: e.tensor_tensor(s2row[:], s2row[:], rrow[:], ALU.subtract),
         inc=1)
    var_done = S.val("vector")
    S.op("scalar", lambda e: e.activation(out=s2row[:], in_=s2row[:], func=AF.Ln,
                                          bias=LNEPS),
         waits=[("vector", var_done)], inc=1)
    S.op("scalar", lambda e: e.activation(out=rrow[:], in_=s2row[:], func=AF.Exp,
                                          scale=-0.5), inc=1)
    rstd_done = S.val("scalar")
    for kc in range(8):
        S.op("tensor", lambda e, kc=kc: e.matmul(
            pB[0:96, 0:1], wx_sb[:, kc * 96:(kc + 1) * 96], ones_sb[:],
            start=(kc == 0), stop=(kc == 7)),
            waits=[("scalar", rstd_done)], inc=(1 if kc == 7 else 0))
    csum_mm = S.val("tensor")
    csum_sb = sb("csum", (96, 1), st=st0)
    S.op("scalar", lambda e: e.activation(out=csum_sb[:], in_=pB[0:96, 0:1], func=AF.Copy),
         waits=[("tensor", csum_mm)], inc=1)
    csum_done = S.val("scalar")
    nmrrow = sb("nmrrow", (1, TC), st=st0)
    S.op("vector", lambda e: e.tensor_tensor(nmrrow[:], mrow[:], rrow[:], ALU.mult),
         waits=[("scalar", rstd_done)], inc=1)
    S.op("vector", lambda e: e.tensor_scalar_mul(out=nmrrow[:], in0=nmrrow[:],
        scalar1=-1.0), inc=1)
    nmr_done = S.val("vector")
    xkc_free2 = xkc_free
    for kc in range(8):
        S.op("sync", lambda e, kc=kc: e.dma_start(out=xkc[:], in_=xo_r[kc]),
             waits=[xkc_free2], inc=16)
        ld = S.val("sync")
        for f in range(4):
            S.op("tensor", lambda e, f=f, kc=kc: e.matmul(
                pA[0:96, f * 512:(f + 1) * 512],
                wx_sb[:, kc * 96:(kc + 1) * 96],
                xkc[:, f * 512:(f + 1) * 512],
                start=(kc == 0), stop=(kc == 7)),
                waits=[("sync", ld)],
                inc=(1 if f == 3 else 0))
        xkc_free2 = ("tensor", S.val("tensor"))
    raw_mm = S.val("tensor")
    S.op("scalar", lambda e: e.activation(out=xp_sb[:], in_=pA[0:96, :], func=AF.Copy),
         waits=[("tensor", raw_mm)], inc=1)
    raw_evac = S.val("scalar")
    for f in range(4):
        S.op("tensor", lambda e, f=f: e.matmul(
            pB[0:96, f * 512:(f + 1) * 512], onesrow_sb[0:1, 0:96],
            rrow[0:1, f * 512:(f + 1) * 512], start=True, stop=True),
            waits=[("scalar", raw_evac)], inc=(1 if f == 3 else 0))
    rbc_mm = S.val("tensor")
    S.op("vector", lambda e: e.tensor_tensor(xp_sb[:], xp_sb[:], pB[0:96, :], ALU.mult),
         waits=[("tensor", rbc_mm), ("vector", nmr_done)], inc=1)
    xp_r_done = S.val("vector")
    for f in range(4):
        S.op("tensor", lambda e, f=f: e.matmul(
            pB[0:96, f * 512:(f + 1) * 512], onesrow_sb[0:1, 0:96],
            nmrrow[0:1, f * 512:(f + 1) * 512], start=True, stop=True),
            waits=[("vector", xp_r_done)], inc=(1 if f == 3 else 0))
    nbc_mm = S.val("tensor")
    S.op("vector", lambda e: e.scalar_tensor_tensor(
        out=xp_sb[:], in0=pB[0:96, :], scalar=csum_sb[:], in1=xp_sb[:],
        op0=ALU.mult, op1=ALU.add),
        waits=[("tensor", nbc_mm), ("scalar", csum_done)], inc=1)
    xp_done_v = S.val("vector")
    xp_evac = xp_done_v
    S.op("sync", lambda e: e.dma_start(out=ag1i_d[0:96, :], in_=xp_sb[:]),
         waits=[("vector", xp_evac)], inc=16)
    S.op("sync", lambda e: e.dma_start(out=ag1i_d[96:97, :], in_=mrow[:]),
         waits=[("scalar", rstd_done)], inc=16)
    S.op("sync", lambda e: e.dma_start(out=ag1i_d[97:98, :], in_=rrow[:]), inc=16)
    ag1_in_done = S.val("sync")
    S.op("gpsimd", lambda e: e.collective_compute(
        "AllGather", ALU.bypass, replica_groups=[list(range(NCORE))],
        ins=[ag1i_d], outs=[ag1o_d]),
        waits=[("sync", ag1_in_done)], inc=1)
    ag1_done = S.val("gpsimd")
    st0.close()

    mixer_ag = []
    if DBG_STAGE >= 1:
        st1 = contextlib.ExitStack()
        dtb = sb("dtb", (DC, L), st=st1)
        dtrq = sb("dtrq", (RK, 1024), st=st1)
        xfh = sb("xfh", (DC, HF), st=st1)
        bmh = sb("bmh", (N, HF), st=st1)
        cmh = sb("cmh", (N, HF), st=st1)
        mrh = sb("mrh", (1, HF), st=st1)
        rrh = sb("rrh", (1, HF), st=st1)
        sp1 = sb("sp1", (DC, 1024), st=st1)
        u_sb = sb("u1", (DC, HF), st=st1)
        x1n = sb("x1n", (DC, HF), st=st1)
        v_sb = sb("v1", (DC, HF), st=st1)
        bmbc = sb("bmbc", (DC, HF), st=st1)
        cmbc = sb("cmbc", (DC, HF), st=st1)
        dA_sb = sb("dA", (DC, HF), st=st1)
        dBx_sb = sb("dBx", (DC, HF), st=st1)
        h_sb = sb("h1", (DC, HF), st=st1)
        hc_sb = sb("hc", (DC, HF), st=st1)
        yfh = sb("yfh", (DC, HF), st=st1)
        hlast = sb("hlast", (DC, NG), st=st1)
        pza = ps("pza", (DC, 1024), st=st1)
        pzb = ps("pzb", (DC, 1024), st=st1)
        pyy = ps("pyy", (DC, HF), st=st1)

        pza_cons = ("sync", 0)
        pzb_cons = ("sync", 0)
        pyy_cons = ("sync", 0)
        dA_cons = ("sync", 0)
        dtrq_free = ("sync", 0)
        sp1_cons = ("sync", 0)
        xfh_free = ("sync", 0)
        bmh_free = ("sync", 0)
        yfh_free = ("sync", 0)

        for b in range(B):
            for q4 in range(4):
                r = 2 * b + q4 // 2
                qo = (q4 % 2) * 1024
                S.op("sync", lambda e, r=r, qo=qo: e.dma_start(
                    out=dtrq[:], in_=ag1o_d[r * 98: r * 98 + RK, qo:qo + 1024]),
                    waits=[dtrq_free, ("gpsimd", ag1_done)], inc=16)
                ld = S.val("sync")
                for f in range(2):
                    S.op("tensor", lambda e, f=f: e.matmul(
                        pza[:, f * 512:(f + 1) * 512], wdt_sb[:],
                        dtrq[:, f * 512:(f + 1) * 512],
                        start=True, stop=True),
                        waits=[("sync", ld), ("scalar", asel_ready)] +
                              ([pza_cons] if pza_cons[1] else []),
                        inc=(1 if f == 1 else 0))
                mmv = S.val("tensor")
                dtrq_free = ("tensor", mmv)
                S.op("scalar", lambda e: e.activation(
                    out=sp1[:], in_=pza[:], func=AF.Abs, bias=bdt_sb[:]),
                    waits=[("tensor", mmv)], inc=1, tag="dts")
                S.op("scalar", lambda e, q4=q4: e.activation(
                    out=dtb[:, q4 * 1024:(q4 + 1) * 1024], in_=pza[:],
                    func=AF.Relu, bias=bdt_sb[:]), inc=1, tag="dts")
                pza_cons = ("scalar", S.val("scalar"))
                S.op("scalar", lambda e: e.activation(
                    out=sp1[:], in_=sp1[:], func=AF.Exp, scale=-1.0), inc=1, tag="dts")
                S.op("scalar", lambda e: e.activation(
                    out=sp1[:], in_=sp1[:], func=AF.Ln, bias=1.0), inc=1, tag="dts")
                sp_fin = S.val("scalar")
                S.op("vector", lambda e, q4=q4: e.tensor_tensor(
                    dtb[:, q4 * 1024:(q4 + 1) * 1024],
                    dtb[:, q4 * 1024:(q4 + 1) * 1024], sp1[:], ALU.add),
                    waits=[("scalar", sp_fin)], inc=1, tag="dts")
                sp1_cons = ("vector", S.val("vector"))
            dt_done = S.val("vector")

            for hf in range(2):
                r = 2 * b + hf
                S.op("sync", lambda e, b=b, hf=hf: e.dma_start(
                    out=xfh[:], in_=xf_d[:, b * L + hf * HF: b * L + (hf + 1) * HF]),
                    waits=[xfh_free] if xfh_free[1] else [], inc=16)
                S.op("sync", lambda e, r=r: e.dma_start(
                    out=bmh[:], in_=ag1o_d[r * 98 + 64: r * 98 + 80, :]),
                    waits=[bmh_free] if bmh_free[1] else [], inc=16)
                S.op("sync", lambda e, r=r: e.dma_start(
                    out=cmh[:], in_=ag1o_d[r * 98 + 80: r * 98 + 96, :]), inc=16)
                S.op("sync", lambda e, r=r: e.dma_start(
                    out=mrh[:], in_=ag1o_d[r * 98 + 96: r * 98 + 97, :]), inc=16)
                S.op("sync", lambda e, r=r: e.dma_start(
                    out=rrh[:], in_=ag1o_d[r * 98 + 97: r * 98 + 98, :]), inc=16)
                h_loads = S.val("sync")

                for q in range(2):
                    pz = pza if q == 0 else pzb
                    pzc = pza_cons if q == 0 else pzb_cons
                    for f in range(2):
                        S.op("tensor", lambda e, q=q, f=f, pz=pz: e.matmul(
                            pz[:, f * 512:(f + 1) * 512], onesrow_sb[:],
                            mrh[0:1, q * 1024 + f * 512: q * 1024 + (f + 1) * 512],
                            start=True, stop=True),
                            waits=[("sync", h_loads), ("vector", dt_done)] +
                                  ([pzc] if pzc[1] else []),
                            inc=(1 if f == 1 else 0))
                    mmv = S.val("tensor")
                    S.op("vector", lambda e, q=q, pz=pz: e.tensor_tensor(
                        u_sb[:, q * 1024:(q + 1) * 1024],
                        xfh[:, q * 1024:(q + 1) * 1024],
                        pz[:], ALU.subtract),
                        waits=[("tensor", mmv)], inc=1)
                u_done = S.val("vector")
                for q in range(2):
                    pz = pza if q == 0 else pzb
                    for f in range(2):
                        S.op("tensor", lambda e, q=q, f=f, pz=pz: e.matmul(
                            pz[:, f * 512:(f + 1) * 512], onesrow_sb[:],
                            rrh[0:1, q * 1024 + f * 512: q * 1024 + (f + 1) * 512],
                            start=True, stop=True),
                            waits=[("vector", u_done)],
                            inc=(1 if f == 1 else 0))
                    mmv = S.val("tensor")
                    S.op("vector", lambda e, q=q, pz=pz: e.tensor_tensor(
                        x1n[:, q * 1024:(q + 1) * 1024],
                        u_sb[:, q * 1024:(q + 1) * 1024], pz[:], ALU.mult),
                        waits=[("tensor", mmv)], inc=1)
                S.op("vector", lambda e, hf=hf: e.tensor_tensor(
                    v_sb[:], x1n[:], dtb[:, hf * HF:(hf + 1) * HF], ALU.mult), inc=1)
                v_done = S.val("vector")
                pza_cons = ("vector", v_done)
                pzb_cons = ("vector", v_done)
                xfh_free = ("vector", u_done)

                for (dst, srcrow) in ((bmbc, bmh), (cmbc, cmh)):
                    for q in range(2):
                        pz = pza if q == 0 else pzb
                        pzc = pza_cons if q == 0 else pzb_cons
                        for f in range(2):
                            S.op("tensor",
                                 lambda e, q=q, f=f, pz=pz, s=srcrow: e.matmul(
                                     pz[:, f * 512:(f + 1) * 512], bsel_sb[:],
                                     s[:, q * 1024 + f * 512: q * 1024 + (f + 1) * 512],
                                     start=True, stop=True),
                                 waits=[("vector", v_done), pzc],
                                 inc=(1 if f == 1 else 0))
                        mmv = S.val("tensor")
                        S.op("scalar", lambda e, d=dst, q=q, pz=pz: e.activation(
                            out=d[:, q * 1024:(q + 1) * 1024], in_=pz[:], func=AF.Copy),
                            waits=[("tensor", mmv)], inc=1, tag="bmcev")
                        if q == 0:
                            pza_cons = ("scalar", S.val("scalar"))
                        else:
                            pzb_cons = ("scalar", S.val("scalar"))
                bc_ready = S.val("scalar")
                bmh_free = ("tensor", S.val("tensor"))

                yred_done_t = 0
                for g in range(NG):
                    for q in range(2):
                        for f in range(2):
                            S.op("tensor", lambda e, g=g, hf=hf, q=q, f=f: e.matmul(
                                pza[:, f * 512:(f + 1) * 512],
                                asel_sb[:, g * DC:(g + 1) * DC],
                                dtb[:, hf * HF + q * 1024 + f * 512:
                                    hf * HF + q * 1024 + (f + 1) * 512],
                                start=True, stop=True),
                                waits=[("scalar", bc_ready), pza_cons],
                                inc=(1 if f == 1 else 0), tag="zmm")
                        zmm = S.val("tensor")
                        S.op("scalar", lambda e, q=q: e.activation(
                            out=dA_sb[:, q * 1024:(q + 1) * 1024], in_=pza[:],
                            func=AF.Exp, scale=-1.0),
                            waits=[("tensor", zmm)] + ([dA_cons] if dA_cons[1] else []),
                            inc=1, tag="exp")
                        pza_cons = ("scalar", S.val("scalar"))
                        for f in range(2):
                            S.op("tensor", lambda e, g=g, q=q, f=f: e.matmul(
                                pzb[:, f * 512:(f + 1) * 512],
                                vsel_sb[:, g * DC:(g + 1) * DC],
                                v_sb[:, q * 1024 + f * 512: q * 1024 + (f + 1) * 512],
                                start=True, stop=True),
                                waits=[("scalar", bc_ready), pzb_cons],
                                inc=(1 if f == 1 else 0), tag="vmm")
                        vmm = S.val("tensor")
                        S.op("vector", lambda e, q=q: e.tensor_tensor(
                            dBx_sb[:, q * 1024:(q + 1) * 1024], pzb[:],
                            bmbc[:, q * 1024:(q + 1) * 1024], ALU.mult),
                            waits=[("tensor", vmm)], inc=1)
                        pzb_cons = ("vector", S.val("vector"))
                    exp_all = S.val("scalar")
                    if hf == 0:
                        S.op("vector", lambda e: e.tensor_tensor_scan(
                            h_sb[:], dA_sb[:], dBx_sb[:], 0.0, ALU.mult, ALU.add),
                            waits=[("scalar", exp_all)], inc=1)
                    else:
                        S.op("vector", lambda e, g=g: e.tensor_tensor_scan(
                            h_sb[:], dA_sb[:], dBx_sb[:], hlast[:, g:g + 1],
                            ALU.mult, ALU.add),
                            waits=[("scalar", exp_all)], inc=1)
                    dA_cons = ("vector", S.val("vector"))
                    scan_v = S.val("vector")
                    S.op("scalar", lambda e, g=g: e.activation(
                        out=hlast[:, g:g + 1], in_=h_sb[:, HF - 1:HF], func=AF.Copy),
                        waits=[("vector", scan_v)], inc=1, tag="hl")
                    S.op("vector", lambda e: e.tensor_tensor(
                        hc_sb[:], h_sb[:], cmbc[:], ALU.mult),
                        waits=([("tensor", yred_done_t)] if yred_done_t else []),
                        inc=1)
                    hc_done = S.val("vector")
                    for f in range(4):
                        S.op("tensor", lambda e, g=g, f=f: e.matmul(
                            pyy[:, f * 512:(f + 1) * 512],
                            rsel_sb[:, g * DC:(g + 1) * DC],
                            hc_sb[:, f * 512:(f + 1) * 512],
                            start=(g == 0), stop=(g == NG - 1)),
                            waits=[("vector", hc_done)] +
                                  ([pyy_cons] if (g == 0 and pyy_cons[1]) else []),
                            inc=(1 if f == 3 else 0), tag="yred")
                    yred_done_t = S.val("tensor")
                S.op("vector", lambda e: e.scalar_tensor_tensor(
                    out=yfh[:], in0=x1n[:], scalar=dsk_sb[:], in1=pyy[:],
                    op0=ALU.mult, op1=ALU.add),
                    waits=[("tensor", yred_done_t)] +
                          ([yfh_free] if yfh_free[1] else []), inc=1)
                pyy_cons = ("vector", S.val("vector"))
                yh_done = S.val("vector")
                S.op("sync", lambda e, b=b, hf=hf: e.dma_start(
                    out=ybin[b][:, hf * HF:(hf + 1) * HF], in_=yfh[:]),
                    waits=[("vector", yh_done)], inc=16)
                yfh_free = ("sync", S.val("sync"))
            yb_st = S.val("sync")
            S.op("gpsimd", lambda e, b=b: e.collective_compute(
                "AllGather", ALU.bypass, replica_groups=[list(range(NCORE))],
                ins=[ybin[b]], outs=[yago[b]]),
                waits=[("sync", yb_st)], inc=1)
            mixer_ag.append(S.val("gpsimd"))
        st1.close()
    stM.close()

    out_written = False
    if DBG_STAGE >= 2:
        mix_done = S.val("gpsimd")
        st2 = contextlib.ExitStack()
        ykc = sb("ykc", (DC, TC), st=st2)
        xkc2 = sb("xkc2", (DC, TC), st=st2)
        skc2 = sb("skc2", (DC, TC), st=st2)
        m2row = sb("m2row", (1, TC), st=st2)
        r2row = sb("r2row", (1, TC), st=st2)
        s22 = sb("s22", (1, TC), st=st2)
        m3row = sb("m3row", (1, TC), st=st2)
        r3row = sb("r3row", (1, TC), st=st2)
        s23 = sb("s23", (1, TC), st=st2)
        yrows_sb = sb("yrows_s", (DC, 8), I32, st=st2)
        p2a = ps("p2a", (DC, 1024), st=st2)
        p2b = ps("p2b", (DC, 1024), st=st2)
        p2c = ps("p2c", (DC, HF), st=st2)

        yago_half = yago_all.rearrange("a b -> (a b)").rearrange("(r w) -> r w", w=HF)
        S.op("sync", lambda e: e.dma_start(out=yrows_sb[:], in_=yrows_d), inc=16)
        yr_ld = S.val("sync")

        def ln_stats(src_tile_loader, mrowo, s2o, rrowo, base_wait):
            free_tracker = [base_wait]
            sq_cons = [("sync", 0)]
            for kc in range(8):
                ld = src_tile_loader(kc, free_tracker[0])
                S.op("vector", lambda e: e.tensor_tensor(skc2[:], ykc[:], ykc[:], ALU.mult),
                     waits=[ld] + ([sq_cons[0]] if sq_cons[0][1] else []), inc=1)
                sqd = S.val("vector")
                for f in range(4):
                    S.op("tensor", lambda e, f=f, kc=kc: e.matmul(
                        p2c[0:1, f * 512:(f + 1) * 512], ones_sb[:],
                        ykc[:, f * 512:(f + 1) * 512],
                        start=(kc == 0), stop=(kc == 7)), waits=[ld])
                for f in range(4):
                    dst = (p2a if f < 2 else p2b)
                    fo = (f % 2) * 512
                    S.op("tensor", lambda e, f=f, kc=kc, dst=dst, fo=fo: e.matmul(
                        dst[0:1, fo:fo + 512],
                        ones_sb[:], skc2[:, f * 512:(f + 1) * 512],
                        start=(kc == 0), stop=(kc == 7)),
                        waits=[("vector", sqd)], inc=(1 if f == 3 else 0))
                free_tracker[0] = ("tensor", S.val("tensor"))
                sq_cons[0] = ("tensor", S.val("tensor"))
            smm = S.val("tensor")
            S.op("scalar", lambda e: e.activation(out=mrowo[:], in_=p2c[0:1, :],
                                                  func=AF.Copy, scale=1.0 / D),
                 waits=[("tensor", smm)], inc=1)
            S.op("scalar", lambda e: e.activation(out=s2o[:, 0:1024], in_=p2a[0:1, :],
                                                  func=AF.Copy, scale=1.0 / D), inc=1)
            S.op("scalar", lambda e: e.activation(out=s2o[:, 1024:2048], in_=p2b[0:1, :],
                                                  func=AF.Copy, scale=1.0 / D), inc=1)
            sev = S.val("scalar")
            S.op("vector", lambda e: e.tensor_tensor(rrowo[:], mrowo[:], mrowo[:], ALU.mult),
                 waits=[("scalar", sev)], inc=1)
            S.op("vector", lambda e: e.tensor_tensor(s2o[:], s2o[:], rrowo[:], ALU.subtract),
                 inc=1)
            vd = S.val("vector")
            S.op("scalar", lambda e: e.activation(out=s2o[:], in_=s2o[:], func=AF.Ln,
                                                  bias=LNEPS),
                 waits=[("vector", vd)], inc=1)
            S.op("scalar", lambda e: e.activation(out=rrowo[:], in_=s2o[:], func=AF.Exp,
                                                  scale=-0.5), inc=1)
            return ("scalar", S.val("scalar"))

        def yload(kc, free_w):
            S.op("gpsimd", lambda e, kc=kc: e.indirect_dma_start(
                out=ykc[:], out_offset=None, in_=yago_half,
                in_offset=IndirectOffsetOnAxis(ap=yrows_sb[:, kc:kc + 1], axis=0)),
                waits=[("sync", yr_ld), ("gpsimd", mix_done)] +
                      ([free_w] if free_w[1] else []), inc=16)
            return ("gpsimd", S.val("gpsimd"))

        ln2_done = ln_stats(yload, m2row, s22, r2row, ("sync", 0))
        ln2_t = ("tensor", S.val("tensor"))
        if DBG_SUB == 1:
            dl = yload(5, ln2_done)
            S.op("sync", lambda e: e.dma_start(
                out=out_d.rearrange("a b -> (a b)")[0:DC * TC],
                in_=ykc[:]), waits=[dl, ln2_done], inc=16)
            S.op("sync", lambda e: e.dma_start(
                out=out_d.rearrange("a b -> (a b)")[DC * TC:DC * TC + TC],
                in_=m2row[:]), inc=16)
            S.op("sync", lambda e: e.dma_start(
                out=out_d.rearrange("a b -> (a b)")[DC * TC + TC:DC * TC + 2 * TC],
                in_=r2row[:]), inc=16)
            fin1 = S.val("sync")
            S.ops.append(("gpsimd", lambda e: None, [("sync", fin1)], 0))
            out_written = True

        P2GO = DBG_SUB
        x1_sp = ("sync", 0)
        ykc_free = ln2_t
        pz_free = ("sync", 0)
        for kc in (range(8) if P2GO >= 2 else []):
            yw = yload(kc, ykc_free)
            S.op("sync", lambda e, kc=kc: e.dma_start(out=xkc2[:], in_=xo_r[kc]),
                 waits=[x1_sp] if x1_sp[1] else [], inc=16)
            xw = S.val("sync")
            for q in range(2):
                pz = p2a if q == 0 else p2b
                for f in range(2):
                    S.op("tensor", lambda e, q=q, f=f, pz=pz: e.matmul(
                        pz[:, f * 512:(f + 1) * 512], onesrow_sb[:],
                        m2row[0:1, q * 1024 + f * 512: q * 1024 + (f + 1) * 512],
                        start=True, stop=True),
                        waits=[ln2_done] + ([pz_free] if pz_free[1] else []),
                        inc=(1 if f == 1 else 0))
                mmv = S.val("tensor")
                S.op("vector", lambda e, q=q, pz=pz: e.tensor_tensor(
                    skc2[:, q * 1024:(q + 1) * 1024], ykc[:, q * 1024:(q + 1) * 1024],
                    pz[:], ALU.subtract), waits=[yw, ("tensor", mmv)], inc=1)
            sd = S.val("vector")
            for q in range(2):
                pz = p2a if q == 0 else p2b
                for f in range(2):
                    S.op("tensor", lambda e, q=q, f=f, pz=pz: e.matmul(
                        pz[:, f * 512:(f + 1) * 512], onesrow_sb[:],
                        r2row[0:1, q * 1024 + f * 512: q * 1024 + (f + 1) * 512],
                        start=True, stop=True),
                        waits=[("vector", sd)], inc=(1 if f == 1 else 0))
                mmv = S.val("tensor")
                S.op("vector", lambda e, q=q, pz=pz: e.tensor_tensor(
                    skc2[:, q * 1024:(q + 1) * 1024], skc2[:, q * 1024:(q + 1) * 1024],
                    pz[:], ALU.mult), waits=[("tensor", mmv)], inc=1)
            S.op("vector", lambda e: e.tensor_tensor(ykc[:], skc2[:], xkc2[:], ALU.add),
                 waits=[("sync", xw)], inc=1)
            x1d = S.val("vector")
            pz_free = ("vector", x1d)
            S.op("sync", lambda e, kc=kc: e.dma_start(
                out=x1f_d[kc * DC:(kc + 1) * DC, :], in_=ykc[:]),
                waits=[("vector", x1d)], inc=16)
            x1_sp = ("sync", S.val("sync"))
            ykc_free = x1_sp
        x1_spilled = S.val("sync")

        def x1load(kc, free_w):
            S.op("sync", lambda e, kc=kc: e.dma_start(
                out=ykc[:], in_=x1f_d[kc * DC:(kc + 1) * DC, :]),
                waits=[("sync", x1_spilled)] + ([free_w] if free_w[1] else []), inc=16)
            return ("sync", S.val("sync"))
        ln3_done = ln_stats(x1load, m3row, s23, r3row, ykc_free) \
            if P2GO >= 3 else ("sync", 0)
        ln3_t = ("tensor", S.val("tensor"))

        lg_started = False
        ln3_sp = ("sync", 0)
        ykc_free = ln3_t
        skc2_guard = []
        pz_free = ("sync", 0)
        for kc in (range(8) if P2GO >= 3 else []):
            lw = x1load(kc, ykc_free)
            for q in range(2):
                pz = p2a if q == 0 else p2b
                for f in range(2):
                    S.op("tensor", lambda e, q=q, f=f, pz=pz: e.matmul(
                        pz[:, f * 512:(f + 1) * 512], onesrow_sb[:],
                        m3row[0:1, q * 1024 + f * 512: q * 1024 + (f + 1) * 512],
                        start=True, stop=True),
                        waits=[ln3_done] + ([pz_free] if pz_free[1] else []),
                        inc=(1 if f == 1 else 0))
                mmv = S.val("tensor")
                S.op("vector", lambda e, q=q, pz=pz: e.tensor_tensor(
                    skc2[:, q * 1024:(q + 1) * 1024], ykc[:, q * 1024:(q + 1) * 1024],
                    pz[:], ALU.subtract),
                    waits=[lw, ("tensor", mmv)] + skc2_guard, inc=1)
            sd = S.val("vector")
            for q in range(2):
                pz = p2a if q == 0 else p2b
                for f in range(2):
                    S.op("tensor", lambda e, q=q, f=f, pz=pz: e.matmul(
                        pz[:, f * 512:(f + 1) * 512], onesrow_sb[:],
                        r3row[0:1, q * 1024 + f * 512: q * 1024 + (f + 1) * 512],
                        start=True, stop=True),
                        waits=[("vector", sd)], inc=(1 if f == 1 else 0))
                mmv = S.val("tensor")
                S.op("vector", lambda e, q=q, pz=pz: e.tensor_tensor(
                    skc2[:, q * 1024:(q + 1) * 1024], skc2[:, q * 1024:(q + 1) * 1024],
                    pz[:], ALU.mult), waits=[("tensor", mmv)], inc=1)
            lnd = S.val("vector")
            pz_free = ("vector", lnd)
            ykc_free = ("vector", lnd)
            S.op("sync", lambda e, kc=kc: e.dma_start(
                out=ln3f_d[kc * DC:(kc + 1) * DC, :], in_=skc2[:]),
                waits=[("vector", lnd)], inc=16)
            ln3_sp = ("sync", S.val("sync"))
            for f in range(4):
                S.op("tensor", lambda e, f=f, kc=kc: e.matmul(
                    p2c[0:8, f * 512:(f + 1) * 512], rwT_sb[:, kc * 8:(kc + 1) * 8],
                    skc2[:, f * 512:(f + 1) * 512],
                    start=(kc == 0), stop=(kc == 7)),
                    waits=[("vector", lnd)],
                    inc=(1 if f == 3 else 0))
            skc2_guard = [("sync", ln3_sp[1]), ("tensor", S.val("tensor"))]
        lg_mm = S.val("tensor")
        ln3_spilled = S.val("sync")
        lgf = sb("lgf", (8, TC), st=st2)
        S.op("scalar", lambda e: e.activation(out=lgf[:], in_=p2c[0:8, :], func=AF.Copy),
             waits=[("tensor", lg_mm)], inc=1)
        lgc = S.val("scalar")
        S.op("vector", lambda e: e.tensor_scalar(out=lgf[:], in0=lgf[:],
                                                 scalar1=rb_sb[:], scalar2=None,
                                                 op0=ALU.add),
             waits=[("scalar", lgc)], inc=1)
        lg_evac_v = S.val("vector")

        ln3tokb = sb("ln3tokb", (DC, NJ * 1024), BF16, st=st2)
        lnall = sb("lnall", (DC, 8 * TC), st=st2)
        trow_t = sb("trow_t", (DC, 1024), st=st2)
        for (srcd, dstd, mkbf) in (((ln3f_d, ln3tok_d, True), (x1f_d, x1tok_d, False))
                                   if P2GO >= 4 else ()):
            lds = []
            for kc in range(8):
                S.op("sync", lambda e, kc=kc, srcd=srcd: e.dma_start(
                    out=lnall[:, kc * TC:(kc + 1) * TC],
                    in_=srcd[kc * DC:(kc + 1) * DC, :]),
                    waits=[("sync", ln3_spilled)] + ([lnall_free] if srcd is x1f_d else []),
                    inc=16)
            all_ld = S.val("sync")
            trow_free = ("sync", 0)
            psum_free = ("sync", 0)
            for tj in range(NJ):
                for half4 in range(2):
                    for k4 in range(4):
                        kc = half4 * 4 + k4
                        S.op("tensor", lambda e, kc=kc, tj=tj, k4=k4: e.transpose(
                            p2c[:, k4 * 128:(k4 + 1) * 128],
                            lnall[:, kc * TC + tj * 128: kc * TC + (tj + 1) * 128],
                            ident_sb[:]),
                            waits=[("sync", all_ld)] +
                                  ([psum_free] if (k4 == 0 and psum_free[1]) else []),
                            inc=(1 if k4 == 3 else 0))
                    tmm = S.val("tensor")
                    S.op("scalar", lambda e, half4=half4: e.activation(
                        out=trow_t[:, half4 * 512:(half4 + 1) * 512],
                        in_=p2c[:, 0:512], func=AF.Copy),
                        waits=[("tensor", tmm)] +
                              ([trow_free] if (half4 == 0 and trow_free[1]) else []),
                        inc=1)
                    psum_free = ("scalar", S.val("scalar"))
                if mkbf:
                    S.op("scalar", lambda e, tj=tj: e.activation(
                        out=ln3tokb[:, tj * 1024:(tj + 1) * 1024], in_=trow_t[:],
                        func=AF.Copy), inc=1)
                ev = S.val("scalar")
                S.op("sync", lambda e, tj=tj, dstd=dstd: e.dma_start(
                    out=dstd[tj * DC:(tj + 1) * DC, :], in_=trow_t[:]),
                    waits=[("scalar", ev)], inc=16)
                trow_free = ("sync", S.val("sync"))
            lnall_free = ("sync", S.val("sync"))
        tok_spilled = S.val("sync")

        Ltok = sb("Ltok", (DC, DC), st=st2)
        T1 = sb("T1", (DC, DC), st=st2)
        T2 = sb("T2", (DC, DC), st=st2)
        T4 = sb("T4", (DC, DC), st=st2)
        T5 = sb("T5", (DC, DC), st=st2)
        T6 = sb("T6", (DC, DC), st=st2)
        posw = [sb(f"posw{k}", (DC, DC), st=st2) for k in range(2)]
        posk = [sb(f"posk{k}", (DC, DC), st=st2) for k in range(2)]
        ohv = [sb(f"ohv{k}", (DC, DC), st=st2) for k in range(2)]
        ii = [sb(f"ii{k}", (DC, 16), st=st2) for k in range(2)]
        ww = [sb(f"ww{k}", (DC, 16), st=st2) for k in range(2)]
        vld = [sb(f"vld{k}", (DC, 16), st=st2) for k in range(2)]
        keep = [sb(f"keep{k}", (DC, 16), st=st2) for k in range(2)]
        wgt = [sb(f"wgt{k}", (DC, 16), st=st2) for k in range(2)]
        rowI = [sb(f"rowI{k}", (DC, 16), I32, st=st2) for k in range(2)]
        zrowI = [sb(f"zrowI{k}", (DC, 16), I32, st=st2) for k in range(2)]
        wsum = sb("wsum", (DC, 16), st=st2)
        wfin = sb("wfin", (DC, 16), st=st2)
        sm1 = sb("sm1", (DC, 16), st=st2)
        sm2 = sb("sm2", (DC, 16), st=st2)
        sm3 = sb("sm3", (DC, 16), st=st2)
        trowbuf = sb("trowbuf", (1, 132), st=st2)
        trow2 = sb("trow2", (1, 132), st=st2)
        cntin = sb("cntin", (1, 128), st=st2)
        cnt_sb = sb("cnt_sb", (8, 128), st=st2)
        baserow = sb("baserow", (1, 16), st=st2)
        jrow = [sb(f"jrow{k}", (1, 128), st=st2) for k in range(2)]

        for tj in (range(NJ) if P2GO >= 5 else []):
            S.op("tensor", lambda e, tj=tj: e.transpose(
                p2c[:, 8 * tj: 8 * tj + 8], lgf[:, tj * 128:(tj + 1) * 128],
                ident_sb[0:8, 0:8]),
                waits=[("vector", lg_evac_v), psum_free],
                inc=(1 if tj == NJ - 1 else 0))
        ltmm = S.val("tensor")
        S.op("scalar", lambda e: e.activation(out=Ltok[:], in_=p2c[:, 0:128],
                                              func=AF.Copy),
             waits=[("tensor", ltmm)], inc=1)
        lt_done = ("scalar", S.val("scalar"))

        ROUTE_ON = P2GO >= 5
        def v16(t):
            return t[:].rearrange("p (j e) -> p j e", e=8)

        def bc16(t):
            return t[:].unsqueeze(-1).broadcast_to([DC, 16, 8])

        V = "vector"
        route_done = ("sync", 0)
        jnk = sb("jnk", (DC, DC), st=st2)

        def JF():
            S.op(V, lambda e: e.tensor_tensor(jnk[:], jnk[:], jnk[:], ALU.mult), inc=0)
        if ROUTE_ON:
            S.op(V, lambda e: e.memset(jnk[:], 1.0), inc=0)
            S.op(V, lambda e: e.tensor_reduce(out=sm1[:].unsqueeze(-1),
                                              in_=v16(Ltok), axis=AX.X, op=ALU.max),
                 waits=[lt_done], inc=1)
            S.op(V, lambda e: e.tensor_tensor(v16(T1), v16(Ltok), bc16(sm1), ALU.subtract),
                 inc=1)
            tv = S.val(V)
            S.op("scalar", lambda e: e.activation(out=T1[:], in_=T1[:], func=AF.Exp),
                 waits=[("vector", tv)], inc=1)
            ex = S.val("scalar")
            S.op(V, lambda e: e.tensor_reduce(out=wsum[:].unsqueeze(-1),
                                              in_=v16(T1), axis=AX.X, op=ALU.add),
                 waits=[("scalar", ex)], inc=1)
            rs_v = S.val(V)
            S.op("scalar", lambda e: e.activation(out=sm2[:], in_=wsum[:], func=AF.Ln),
                 waits=[("vector", rs_v)], inc=1)
            S.op("scalar", lambda e: e.activation(out=sm2[:], in_=sm2[:], func=AF.Exp,
                                                  scale=-1.0), inc=1)
            rs_s = S.val("scalar")
            S.op(V, lambda e: e.tensor_tensor(v16(T1), v16(T1), bc16(sm2), ALU.mult),
                 waits=[("scalar", rs_s)], inc=1)
            S.op(V, lambda e: e.tensor_reduce(out=sm1[:].unsqueeze(-1),
                                              in_=v16(T1), axis=AX.X, op=ALU.max), inc=1)
            S.op(V, lambda e: e.tensor_tensor(v16(T2), v16(T1), bc16(sm1), ALU.is_equal),
                 inc=1)
            S.op(V, lambda e: e.tensor_tensor(T4[:], T2[:], m9_sb[:], ALU.mult), inc=1)
            S.op(V, lambda e: e.tensor_scalar_add(out=T4[:], in0=T4[:], scalar1=9.0), inc=1)
            S.op(V, lambda e: e.tensor_reduce(out=ii[0][:].unsqueeze(-1),
                                              in_=v16(T4), axis=AX.X, op=ALU.min), inc=1)
            S.op(V, lambda e: e.tensor_tensor(v16(T2), iotaE_sb[:].rearrange(
                "p (j e) -> p j e", e=8), bc16(ii[0]), ALU.is_equal), inc=1)
            S.op(V, lambda e: e.scalar_tensor_tensor(out=T4[:], in0=T2[:], scalar=-1e30,
                                                     in1=T1[:], op0=ALU.mult, op1=ALU.add),
                 inc=1)
            S.op(V, lambda e: e.tensor_reduce(out=sm2[:].unsqueeze(-1),
                                              in_=v16(T4), axis=AX.X, op=ALU.max), inc=1)
            S.op(V, lambda e: e.tensor_tensor(v16(T5), v16(T4), bc16(sm2), ALU.is_equal),
                 inc=1)
            S.op(V, lambda e: e.tensor_tensor(T6[:], T5[:], m9_sb[:], ALU.mult), inc=1)
            S.op(V, lambda e: e.tensor_scalar_add(out=T6[:], in0=T6[:], scalar1=9.0), inc=1)
            S.op(V, lambda e: e.tensor_reduce(out=ii[1][:].unsqueeze(-1),
                                              in_=v16(T6), axis=AX.X, op=ALU.min), inc=1)
            S.op(V, lambda e: e.tensor_tensor(v16(T5), iotaE_sb[:].rearrange(
                "p (j e) -> p j e", e=8), bc16(ii[1]), ALU.is_equal), inc=1)
            S.op(V, lambda e: e.scalar_tensor_tensor(out=wsum[:], in0=sm1[:], scalar=1e-9,
                                                     in1=sm2[:], op0=ALU.add, op1=ALU.add),
                 inc=1)
            ws_v = S.val(V)
            S.op("scalar", lambda e: e.activation(out=sm3[:], in_=wsum[:], func=AF.Ln),
                 waits=[("vector", ws_v)], inc=1)
            S.op("scalar", lambda e: e.activation(out=sm3[:], in_=sm3[:], func=AF.Exp,
                                                  scale=-1.0), inc=1)
            ws_s = S.val("scalar")
            S.op(V, lambda e: e.tensor_tensor(ww[0][:], sm1[:], sm3[:], ALU.mult),
                 waits=[("scalar", ws_s)], inc=1)
            S.op(V, lambda e: e.tensor_tensor(ww[1][:], sm2[:], sm3[:], ALU.mult), inc=1)
            for k in range(2):
                JF()
                S.op(V, lambda e, k=k: e.tensor_single_scalar(
                    out=vld[k][:], in_=ww[k][:], scalar=1e-9, op=ALU.is_gt), inc=1)
                JF()
                S.op(V, lambda e, k=k: e.tensor_tensor(
                    v16(ohv[k]), v16(T2 if k == 0 else T5), bc16(vld[k]), ALU.mult), inc=1)
            rt_v = S.val(V)

            p2c_cons = ("vector", 0)
            pose_prev = ("scalar", 0)
            for k in (range(2) if P2GO >= 6 else []):
                for tj in range(NJ):
                    S.op("tensor", lambda e, k=k, tj=tj: e.matmul(
                        p2c[:, 512 + 8 * tj: 512 + 8 * tj + 8], tri_sb[:],
                        ohv[k][:, 8 * tj: 8 * tj + 8], start=True, stop=True),
                        waits=[("vector", rt_v)] +
                              ([p2c_cons] if p2c_cons[1] else []) +
                              ([pose_prev] if pose_prev[1] else []),
                        inc=(1 if tj == NJ - 1 else 0), tag="trimm")
                pmm = S.val("tensor")
                S.op("scalar", lambda e, k=k: e.activation(
                    out=posw[k][:], in_=p2c[:, 512:640], func=AF.Copy),
                    waits=[("tensor", pmm)], inc=1)
                pose = S.val("scalar")
                S.op("tensor", lambda e, k=k: e.matmul(
                    p2c[0:1, 128:256], ones_sb[:],
                    ohv[k][:].rearrange("p (j e) -> p e j", e=8), start=True, stop=True),
                    waits=[("scalar", pose)], inc=1)
                tot_mm = S.val("tensor")
                S.op(V, lambda e, k=k: e.memset(trowbuf[:, 0:2], 0.0),
                     waits=[("tensor", tot_mm)], inc=1)
                S.op(V, lambda e, k=k: e.tensor_copy(trowbuf[:, 2:130], p2c[0:1, 128:256]),
                     inc=1)
                S.op(V, lambda e, k=k: e.memset(trow2[:, 0:2], 0.0), inc=1)
                S.op(V, lambda e, k=k: e.tensor_tensor_scan(
                    trow2[:, 2:130], onesrow_sb[:], trowbuf[:, 2:130], 0.0,
                    ALU.mult, ALU.add), inc=1)
                JF()
                S.op(V, lambda e, k=k: e.tensor_tensor(
                    jrow[k][:].rearrange("p (e j) -> p e j", j=16),
                    trow2[:, 1:129].rearrange("p (e j) -> p e j", j=16),
                    trow2[:, 1:129].rearrange("p (e j) -> p e j", j=16)[
                        :, :, 0:1].broadcast_to([1, 8, 16]),
                    ALU.subtract), inc=1)
                S.op(V, lambda e, k=k: e.tensor_tensor(
                    cntin[:, 8 * k: 8 * k + 8],
                    trow2[:, 2:130].rearrange("p (e j) -> p e j", j=16)[:, :, 15],
                    trow2[:, 1:129].rearrange("p (e j) -> p e j", j=16)[:, :, 0],
                    ALU.subtract), inc=1)
            cnt_v = S.val(V)
            if P2GO >= 7:
                S.op("gpsimd", lambda e: e.dma_start(out=cnt_i[:, 0:128], in_=cntin[:]),
                     waits=[("vector", cnt_v)], inc=16)
                cg = S.val("gpsimd")
                S.op("gpsimd", lambda e: e.collective_compute(
                    "AllGather", ALU.bypass, replica_groups=[list(range(NCORE))],
                    ins=[cnt_i], outs=[cnt_o]),
                    waits=[("gpsimd", cg)], inc=1)
                ag3 = S.val("gpsimd")
                S.op("sync", lambda e: e.dma_start(out=cnt_sb[:], in_=cnt_o),
                     waits=[("gpsimd", ag3)], inc=16)
                cnt_ld = S.val("sync")
                S.op("tensor", lambda e: e.matmul(p2c[0:1, 0:8], maskC_sb[:], cnt_sb[:, 0:8],
                                                  start=True, stop=True),
                     waits=[("sync", cnt_ld)], inc=0)
                S.op("tensor", lambda e: e.matmul(p2c[0:1, 8:16], ones_sb[0:8, :],
                                                  cnt_sb[:, 0:8], start=True, stop=False),
                     inc=0)
                S.op("tensor", lambda e: e.matmul(p2c[0:1, 8:16], maskC_sb[:],
                                                  cnt_sb[:, 8:16], start=False, stop=True),
                     inc=1)
                base_mm = S.val("tensor")
                S.op("scalar", lambda e: e.activation(out=baserow[:], in_=p2c[0:1, 0:16],
                                                      func=AF.Copy),
                     waits=[("tensor", base_mm)], inc=1)
                base_ev = S.val("scalar")
                for k in range(2):
                    S.op(V, lambda e, k=k: e.tensor_tensor(
                        jrow[k][:].rearrange("p (e j) -> p e j", j=16),
                        jrow[k][:].rearrange("p (e j) -> p e j", j=16),
                        baserow[:, 8 * k: 8 * k + 8].unsqueeze(-1).broadcast_to([1, 8, 16]), ALU.add),
                        waits=[("scalar", base_ev)], inc=1)
                    jb_v = S.val(V)
                    S.op("tensor", lambda e, k=k: e.matmul(
                        p2c[:, 0:128], onesrow_sb[:],
                        jrow[k][:].rearrange("p (e j) -> p j e", j=16),
                        start=True, stop=True),
                        waits=[("vector", jb_v)], inc=1)
                    jbc_mm = S.val("tensor")
                    S.op(V, lambda e, k=k: e.tensor_tensor(posk[k][:], posw[k][:], p2c[:, 0:128],
                                                           ALU.add),
                         waits=[("tensor", jbc_mm)], inc=1)
                    S.op(V, lambda e, k=k: e.tensor_tensor(T6[:], ohv[k][:], posk[k][:],
                                                           ALU.mult), inc=1)
                    S.op(V, lambda e, k=k: e.tensor_reduce(
                        out=sm1[:].unsqueeze(-1), in_=v16(T6), axis=AX.X,
                        op=ALU.add), inc=1)
                    JF()
                    S.op(V, lambda e, k=k: e.tensor_single_scalar(
                        out=sm2[:], in_=sm1[:], scalar=float(CAP), op=ALU.is_lt), inc=1)
                    JF()
                    S.op(V, lambda e, k=k: e.tensor_tensor(keep[k][:], sm2[:], vld[k][:],
                                                           ALU.mult), inc=1)
                    JF()
                    S.op(V, lambda e, k=k: e.tensor_tensor(wgt[k][:], ww[k][:], keep[k][:],
                                                           ALU.mult), inc=1)
                    JF()
                    S.op(V, lambda e, k=k: e.memset(sm3[:], 0.0), inc=1)
                    JF()
                    for th in (512.0, 1024.0, 1536.0, 2048.0):
                        S.op(V, lambda e, k=k, th=th: e.tensor_single_scalar(
                            out=sm2[:], in_=sm1[:], scalar=th, op=ALU.is_ge), inc=1)
                        JF()
                        S.op(V, lambda e, k=k: e.tensor_tensor(sm3[:], sm3[:], sm2[:],
                                                               ALU.add), inc=1)
                        JF()
                    S.op(V, lambda e, k=k: e.tensor_scalar_mul(out=sm3[:], in0=sm3[:],
                                                               scalar1=3584.0), inc=1)
                    JF()
                    S.op(V, lambda e, k=k: e.scalar_tensor_tensor(
                        out=sm3[:], in0=ii[k][:], scalar=512.0, in1=sm3[:],
                        op0=ALU.mult, op1=ALU.add), inc=1)
                    JF()
                    S.op(V, lambda e, k=k: e.tensor_tensor(sm3[:], sm3[:], sm1[:], ALU.add),
                         inc=1)
                    JF()
                    S.op(V, lambda e, k=k: e.tensor_tensor(sm2[:], sm3[:], keep[k][:],
                                                           ALU.mult), inc=1)
                    JF()
                    S.op(V, lambda e, k=k: e.tensor_copy(zrowI[k][:], sm2[:]), inc=1)
                    JF()
                    S.op(V, lambda e, k=k: e.tensor_scalar_add(out=sm3[:], in0=sm3[:],
                                                               scalar1=-float(TRASH)), inc=1)
                    JF()
                    S.op(V, lambda e, k=k: e.tensor_tensor(sm3[:], sm3[:], keep[k][:],
                                                           ALU.mult), inc=1)
                    JF()
                    S.op(V, lambda e, k=k: e.tensor_scalar_add(out=sm3[:], in0=sm3[:],
                                                               scalar1=float(TRASH)), inc=1)
                    JF()
                    S.op(V, lambda e, k=k: e.tensor_copy(rowI[k][:], sm3[:]), inc=1)
                    JF()
                S.op(V, lambda e: e.tensor_tensor(wfin[:], wgt[0][:], wgt[1][:], ALU.add),
                     inc=1)
                JF()
            route_done = S.val(V)

        smls_sp = S.val("sync")
        if ROUTE_ON and P2GO >= 7:
            for (dst, src) in ((wgtd_d[:, 0:16], wgt[0]), (wgtd_d[:, 16:32], wgt[1]),
                               (wgtd_d[:, 32:48], wfin)):
                S.op("sync", lambda e, dst=dst, src=src: e.dma_start(out=dst, in_=src[:]),
                     waits=[("vector", route_done)], inc=16)
            for (dst, src) in ((zrid_d[:, 0:16], zrowI[0]), (zrid_d[:, 16:32], zrowI[1])):
                S.op("sync", lambda e, dst=dst, src=src: e.dma_start(out=dst, in_=src[:]),
                     waits=[("vector", route_done)], inc=16)
            smls_sp = S.val("sync")

        for k in (range(2) if DBG_STAGE >= 4 else []):
            for tj in range(NJ):
                S.op("gpsimd", lambda e, k=k, tj=tj: e.indirect_dma_start(
                    out=xl_d,
                    out_offset=IndirectOffsetOnAxis(ap=rowI[k][:, tj:tj + 1], axis=0),
                    in_=ln3tokb[:, tj * 1024:(tj + 1) * 1024], in_offset=None),
                    waits=[("vector", route_done), ("gpsimd", memset_done)], inc=16)
        if DBG_STAGE >= 4:
            scfence = sb("scfence", (1, 1024), BF16, st=st2)
            S.op("gpsimd", lambda e: e.dma_start(
                out=scfence[:], in_=xl_d[XROWS:XROWS + 1, :]),
                waits=[("gpsimd", S.val("gpsimd"))], inc=16)
        scat_done = S.val("gpsimd")

        if DBG_STAGE == 2 and not out_written:
            allw = [(s, S.val(s)) for s in ENGS if s != "sync"] + \
                   [("sync", S.val("sync"))]
            dsrc = x1f_d if P2GO <= 2 else (ln3f_d if P2GO == 3 else ln3tok_d)
            S.op("sync", lambda e, dsrc=dsrc: e.dma_start(
                out=out_d.rearrange("a b -> (a b)"),
                in_=dsrc.rearrange("a b -> (a b)")),
                 waits=[(s, v) for (s, v) in allw if v], inc=16)
            out_written = True
        elif DBG_STAGE == 3:
            dbgt = sb("dbgt", (DC, 1024), st=st2)
            S.op(V, lambda e: e.memset(dbgt[:], 0.0), waits=[], inc=1)
            pieces = [rowI[0][:].bitcast(F32), rowI[1][:].bitcast(F32),
                      zrowI[0][:].bitcast(F32), zrowI[1][:].bitcast(F32),
                      wgt[0][:], wgt[1][:], keep[0][:], keep[1][:],
                      ii[0][:], ii[1][:], vld[0][:], vld[1][:],
                      ww[0][:], ww[1][:]]
            for pi, ap in enumerate(pieces):
                S.op(V, lambda e, pi=pi, ap=ap: e.tensor_copy(
                    dbgt[:, pi * 16:(pi + 1) * 16], ap), inc=1)
            wide = [(240, Ltok[:]), (368, ohv[0][:]), (496, posw[0][:]),
                    (624, posk[0][:])]
            for (co, ap) in wide:
                S.op(V, lambda e, co=co, ap=ap: e.tensor_copy(
                    dbgt[:, co:co + 128], ap), inc=1)
            S.op(V, lambda e: e.tensor_copy(dbgt[0:1, 752:880], jrow[0][:]), inc=1)
            S.op(V, lambda e: e.tensor_copy(dbgt[0:1, 880:896], baserow[:]), inc=1)
            S.op(V, lambda e: e.tensor_copy(dbgt[0:1, 896:1024], cntin[:]), inc=1)
            dv = S.val(V)
            S.op("sync", lambda e: e.dma_start(out=out_d[0:DC, 0:1024], in_=dbgt[:]),
                 waits=[("vector", dv), ("gpsimd", scat_done)], inc=16)
            out_written = True
        if DBG_STAGE in (2, 3):
            fin2 = S.val("sync")
            S.ops.append(("gpsimd", lambda e: None, [("sync", fin2)], 0))
        st2.close()

        if DBG_STAGE >= 5 and not out_written:
            st3 = contextlib.ExitStack()
            w1sb = sb("w1sb", (DC, 8 * HID), BF16, st=st3)
            xfe = sb("xfe", (DC, 8 * FCH), BF16, st=st3)
            hsb = sb("hsb", (DC, 32 * FCH), BF16, st=st3)
            w2t = [sb(f"w2t{i}", (DC, 32 * DC), BF16, st=st3) for i in range(2)]
            ztf = sb("ztf", (DC, 8 * FCH), st=st3)
            xst = [sb(f"xst{i}", (DC, D), BF16, st=st3) for i in range(2)]
            xsf = [sb(f"xsf{i}", (DC, D), st=st3) for i in range(2)]
            ztk = [sb(f"ztk{i}", (DC, D), BF16, st=st3) for i in range(2)]
            pf1 = ps("pf1", (DC, 1024), st=st3)
            pf2 = ps("pf2", (DC, 1024), st=st3)
            pt3 = ps("pt3", (DC, 2048), st=st3)

            for _ in range(2):
                S.op("gpsimd", lambda e: e.collective_compute(
                    "AllReduce", ALU.add, replica_groups=[list(range(NCORE))],
                    ins=[barb_d], outs=[barb_d]),
                    waits=[("gpsimd", scat_done)], inc=1)
            rs_done = []
            for f in range(NF):
                S.op("gpsimd", lambda e, f=f: e.collective_compute(
                    "ReduceScatter", ALU.add, replica_groups=[list(range(NCORE))],
                    ins=[xl_d[f * 8 * FCH:(f + 1) * 8 * FCH, :]], outs=[xs_d[f]]),
                    inc=1)
                S.op("gpsimd", lambda e, f=f: e.collective_compute(
                    "AllGather", ALU.bypass, replica_groups=[list(range(NCORE))],
                    ins=[xs_d[f]], outs=[xg_d[f]]), inc=1)
                rs_done.append(S.val("gpsimd"))
            rs_wait = rs_done
            xgr_sb = sb("xgr", (DC, 4), I32, st=st3)
            S.op("sync", lambda e: e.dma_start(out=xgr_sb[:], in_=xgrows_d), inc=16)
            xgr_ld = S.val("sync")

            S.op("sync", lambda e: e.dma_start(
                out=w1sb[:].rearrange("p (kc m) -> p kc m", m=HID),
                in_=w1b16_d.rearrange("(kc p) m -> p kc m", p=DC)),
                waits=[("gpsimd", cast_done), ("gpsimd", scat_done),
                       ("sync", smls_sp)], inc=16)
            w1_ld = S.val("sync")

            xst_free = [("sync", 0), ("sync", 0)]
            xsf_free = [("sync", 0), ("sync", 0)]
            pt_free = [("sync", 0), ("sync", 0)]
            pf1_free = [("sync", 0), ("sync", 0)]
            pf2_free = [("sync", 0), ("sync", 0)]
            w2t_free = [("sync", 0), ("sync", 0)]
            ztk_free = [("sync", 0), ("sync", 0)]
            xfe_free = ("sync", 0)
            hsb_free = ("sync", 0)
            ztf_free = ("sync", 0)
            xfe3 = xfe[:].rearrange("p (kc s) -> p kc s", s=FCH)
            ag_done = []
            for f in range(NF):
                for sj in range(4):
                    pr = sj % 2
                    S.op("gpsimd", lambda e, f=f, sj=sj, pr=pr: e.indirect_dma_start(
                        out=xst[pr][:], out_offset=None, in_=xg_d[f],
                        in_offset=IndirectOffsetOnAxis(
                            ap=xgr_sb[:, sj:sj + 1], axis=0)),
                        waits=[("gpsimd", rs_wait[f]), ("sync", xgr_ld)] +
                              ([xst_free[pr]] if xst_free[pr][1] else []), inc=16)
                    xld = S.val("gpsimd")
                    S.op("scalar", lambda e, pr=pr: e.activation(
                        out=xsf[pr][:], in_=xst[pr][:], func=AF.Copy),
                        waits=[("gpsimd", xld)] +
                              ([xsf_free[pr]] if xsf_free[pr][1] else []), inc=1)
                    xcp = S.val("scalar")
                    xst_free[pr] = ("scalar", xcp)
                    for kc in range(8):
                        S.op("tensor", lambda e, pr=pr, kc=kc: e.transpose(
                            pt3[:, pr * 1024 + kc * DC: pr * 1024 + (kc + 1) * DC],
                            xsf[pr][:, kc * DC:(kc + 1) * DC], ident_sb[:]),
                            waits=[("scalar", xcp)] +
                                  ([pt_free[pr]] if (kc == 0 and pt_free[pr][1]) else []),
                            inc=(1 if kc == 7 else 0))
                    tmm = S.val("tensor")
                    xsf_free[pr] = ("tensor", tmm)
                    S.op("scalar", lambda e, pr=pr, sj=sj: e.activation(
                        out=xfe3[:, :, sj * DC:(sj + 1) * DC],
                        in_=pt3[:, pr * 1024:(pr + 1) * 1024].rearrange(
                            "p (kc s) -> p kc s", s=DC), func=AF.Copy),
                        waits=[("tensor", tmm)] +
                              ([xfe_free] if (sj == 0 and xfe_free[1]) else []), inc=1)
                    pt_free[pr] = ("scalar", S.val("scalar"))
                xfe_ready = S.val("scalar")

                for mt in range(32):
                    pr = mt % 2
                    for kc in range(8):
                        S.op("tensor", lambda e, mt=mt, kc=kc, pr=pr: e.matmul(
                            pf1[:, pr * 512:(pr + 1) * 512],
                            w1sb[:, kc * HID + mt * DC: kc * HID + (mt + 1) * DC],
                            xfe[:, kc * FCH:(kc + 1) * FCH],
                            start=(kc == 0), stop=(kc == 7)),
                            waits=[("scalar", xfe_ready), ("sync", w1_ld)] +
                                  ([pf1_free[pr]] if (kc == 0 and pf1_free[pr][1]) else []),
                            inc=(1 if kc == 7 else 0))
                    fmm = S.val("tensor")
                    S.op("scalar", lambda e, mt=mt, pr=pr: e.activation(
                        out=hsb[:, mt * FCH:(mt + 1) * FCH],
                        in_=pf1[:, pr * 512:(pr + 1) * 512],
                        func=AF.Gelu, bias=w1bias_sb[:, mt:mt + 1]),
                        waits=[("tensor", fmm)] +
                              ([hsb_free] if (mt == 0 and hsb_free[1]) else []), inc=1)
                    pf1_free[pr] = ("scalar", S.val("scalar"))
                gelu_all = S.val("scalar")
                xfe_free = ("tensor", S.val("tensor"))

                for cm in range(8):
                    pr = cm % 2
                    S.op("sync", lambda e, cm=cm, pr=pr: e.dma_start(
                        out=w2t[pr][:].rearrange("p (kh m) -> p kh m", m=DC),
                        in_=w2b16_d[cm]),
                        waits=[("gpsimd", cast_done)] +
                              ([w2t_free[pr]] if w2t_free[pr][1] else []), inc=16)
                    wld = S.val("sync")
                    for kh in range(32):
                        S.op("tensor", lambda e, cm=cm, kh=kh, pr=pr: e.matmul(
                            pf2[:, pr * 512:(pr + 1) * 512],
                            w2t[pr][:, kh * DC:(kh + 1) * DC],
                            hsb[:, kh * FCH:(kh + 1) * FCH],
                            start=(kh == 0), stop=(kh == 31)),
                            waits=[("sync", wld), ("scalar", gelu_all)] +
                                  ([pf2_free[pr]] if (kh == 0 and pf2_free[pr][1]) else []),
                            inc=(1 if kh == 31 else 0))
                    zmm = S.val("tensor")
                    w2t_free[pr] = ("tensor", zmm)
                    S.op("vector", lambda e, cm=cm, pr=pr: e.scalar_tensor_tensor(
                        out=ztf[:, cm * FCH:(cm + 1) * FCH],
                        in0=pf2[:, pr * 512:(pr + 1) * 512],
                        scalar=w2b_sb[:, cm:cm + 1],
                        in1=lss_sb[:, cm:cm + 1].broadcast_to([DC, FCH]),
                        op0=ALU.add, op1=ALU.mult),
                        waits=[("tensor", zmm)] +
                              ([ztf_free] if (cm == 0 and ztf_free[1]) else []), inc=1)
                    pf2_free[pr] = ("vector", S.val("vector"))
                ztf_all = S.val("vector")
                hsb_free = ("tensor", S.val("tensor"))

                for sj in range(4):
                    pr = sj % 2
                    for cm in range(8):
                        S.op("tensor", lambda e, sj=sj, cm=cm, pr=pr: e.transpose(
                            pt3[:, pr * 1024 + cm * DC: pr * 1024 + (cm + 1) * DC],
                            ztf[:, cm * FCH + sj * DC: cm * FCH + (sj + 1) * DC],
                            ident_sb[:]),
                            waits=[("vector", ztf_all)] +
                                  ([pt_free[pr]] if (cm == 0 and pt_free[pr][1]) else []),
                            inc=(1 if cm == 7 else 0))
                    tmm = S.val("tensor")
                    S.op("scalar", lambda e, pr=pr: e.activation(
                        out=ztk[pr][:], in_=pt3[:, pr * 1024:(pr + 1) * 1024],
                        func=AF.Copy),
                        waits=[("tensor", tmm)] +
                              ([ztk_free[pr]] if ztk_free[pr][1] else []), inc=1)
                    pt_free[pr] = ("scalar", S.val("scalar"))
                    zev = S.val("scalar")
                    S.op("sync", lambda e, f=f, sj=sj, pr=pr: e.dma_start(
                        out=zin_d[f][sj * DC:(sj + 1) * DC, :], in_=ztk[pr][:]),
                        waits=[("scalar", zev)], inc=16)
                    ztk_free[pr] = ("sync", S.val("sync"))
                ztf_free = ("tensor", S.val("tensor"))
                zst = S.val("sync")
                S.op("gpsimd", lambda e, f=f: e.collective_compute(
                    "AllGather", ALU.bypass, replica_groups=[list(range(NCORE))],
                    ins=[zin_d[f]], outs=[zr_d[f * 8 * FCH:(f + 1) * 8 * FCH, :]]),
                    waits=[("sync", zst)], inc=1)
                ag_done.append(S.val("gpsimd"))
            S.op("gpsimd", lambda e: e.collective_compute(
                "AllReduce", ALU.add, replica_groups=[list(range(NCORE))],
                ins=[barb_d], outs=[barb_d]), inc=1)
            ag_done.append(S.val("gpsimd"))

            if DBG_STAGE == 9:
                zb9 = sb("zb9", (DC, 1024), BF16, st=st3)
                fb9 = sb("fb9", (DC, 1024), st=st3)
                jobs9 = [(xfe, 0, 0, True), (xfe, 1024, 1, True),
                         (xfe, 2048, 2, True), (xfe, 3072, 3, True),
                         (w1sb, 0, 4, True), (w1sb, 1024, 5, True),
                         (hsb, 0, 6, True), (hsb, 1024, 7, True),
                         (ztf, 0, 8, False), (ztf, 1024, 9, False)]
                first = True
                last = ("sync", S.val("sync"))
                for (src, col, orow, isbf) in jobs9:
                    if isbf:
                        S.op("sync", lambda e, src=src, col=col: e.dma_start(
                            out=zb9[:], in_=src[:, col:col + 1024]),
                            waits=[last], inc=16,
                            tag=("BARRIER" if first else None))
                        zl = S.val("sync")
                        S.op("scalar", lambda e: e.activation(
                            out=fb9[:], in_=zb9[:], func=AF.Copy),
                            waits=[("sync", zl)], inc=1)
                        zc = S.val("scalar")
                        S.op("sync", lambda e, orow=orow: e.dma_start(
                            out=x1f_d[(orow % 8) * DC:((orow % 8) + 1) * DC,
                                      (orow // 8) * 1024:(orow // 8) * 1024 + 1024],
                            in_=fb9[:]),
                            waits=[("scalar", zc)], inc=16)
                    else:
                        S.op("sync", lambda e, src=src, col=col, orow=orow: e.dma_start(
                            out=x1f_d[(orow % 8) * DC:((orow % 8) + 1) * DC,
                                      (orow // 8) * 1024:(orow // 8) * 1024 + 1024],
                            in_=src[:, col:col + 1024]),
                            waits=[last], inc=16,
                            tag=("BARRIER" if first else None))
                    first = False
                    last = ("sync", S.val("sync"))
            st3.close()

            st4 = contextlib.ExitStack()
            zg = [[sb(f"zg{k}_{i}", (DC, D), BF16, st=st4) for i in range(2)]
                  for k in range(2)]
            ltb = [sb(f"ltb{i}", (DC, D), st=st4) for i in range(2)]
            x1b = [sb(f"x1b{i}", (DC, D), st=st4) for i in range(2)]
            accb = [sb(f"acc{i}", (DC, D), st=st4) for i in range(2)]
            outb = [sb(f"outb{i}", (DC, D), st=st4) for i in range(2)]
            sq4 = sb("sq4", (DC, D), st=st4)
            wg_sb = sb("wg4", (DC, 48), st=st4)
            zri_sb = sb("zri4", (DC, 32), I32, st=st4)
            m4 = sb("m4", (DC, 1), st=st4)
            s4 = sb("s4", (DC, 1), st=st4)
            mm4 = sb("mm4", (DC, 1), st=st4)
            ss4 = sb("ss4", (DC, 1), st=st4)
            t4s = sb("t4s", (DC, 1), st=st4)
            var4 = sb("var4", (DC, 1), st=st4)
            rstd4 = sb("rstd4", (DC, 1), st=st4)
            jnk4 = sb("jnk4", (DC, DC), st=st4)

            def JF4():
                S.op("vector", lambda e: e.tensor_tensor(jnk4[:], jnk4[:], jnk4[:],
                                                         ALU.mult), inc=0)

            S.op("sync", lambda e: e.dma_start(out=wg_sb[:], in_=wgtd_d),
                 waits=[("sync", smls_sp)], inc=16, tag="BARRIER")
            S.op("sync", lambda e: e.dma_start(out=zri_sb[:], in_=zrid_d), inc=16)
            p4_ld = S.val("sync")
            S.op("vector", lambda e: e.memset(jnk4[:], 1.0),
                 waits=[("tensor", S.val("tensor")), ("scalar", S.val("scalar"))], inc=1)

            zg_free = [[("sync", 0)] * 2 for _ in range(2)]
            lt_free = [("sync", 0), ("sync", 0)]
            acc_free = [("sync", 0), ("sync", 0)]
            out_free = [("sync", 0), ("sync", 0)]
            agw = ag_done[-1]
            for tj in range(NJ):
                pr = tj % 2
                for k in range(2):
                    S.op("gpsimd", lambda e, k=k, tj=tj, pr=pr: e.indirect_dma_start(
                        out=zg[k][pr][:], out_offset=None, in_=zr_d,
                        in_offset=IndirectOffsetOnAxis(
                            ap=zri_sb[:, k * 16 + tj: k * 16 + tj + 1], axis=0)),
                        waits=[("gpsimd", agw), ("sync", p4_ld)] +
                              ([zg_free[k][pr]] if zg_free[k][pr][1] else []), inc=16)
                g2 = S.val("gpsimd")
                S.op("sync", lambda e, tj=tj, pr=pr: e.dma_start(
                    out=ltb[pr][:], in_=ln3tok_d[tj * DC:(tj + 1) * DC, :]),
                    waits=[("sync", tok_spilled)] +
                          ([lt_free[pr]] if lt_free[pr][1] else []), inc=16)
                S.op("sync", lambda e, tj=tj, pr=pr: e.dma_start(
                    out=x1b[pr][:], in_=x1tok_d[tj * DC:(tj + 1) * DC, :]), inc=16)
                lld = S.val("sync")
                S.op("vector", lambda e, tj=tj, pr=pr: e.tensor_scalar(
                    out=accb[pr][:], in0=zg[0][pr][:],
                    scalar1=wg_sb[:, tj:tj + 1], scalar2=None, op0=ALU.mult),
                    waits=[("gpsimd", g2), ("sync", lld)] +
                          ([acc_free[pr]] if acc_free[pr][1] else []), inc=1)
                S.op("vector", lambda e, tj=tj, pr=pr: e.scalar_tensor_tensor(
                    out=accb[pr][:], in0=zg[1][pr][:],
                    scalar=wg_sb[:, 16 + tj:17 + tj], in1=accb[pr][:],
                    op0=ALU.mult, op1=ALU.add), inc=1)
                S.op("vector", lambda e, tj=tj, pr=pr: e.scalar_tensor_tensor(
                    out=accb[pr][:], in0=ltb[pr][:],
                    scalar=wg_sb[:, 32 + tj:33 + tj], in1=accb[pr][:],
                    op0=ALU.mult, op1=ALU.add), inc=1)
                acc_d = S.val("vector")
                zg_free[0][pr] = ("vector", acc_d)
                zg_free[1][pr] = ("vector", acc_d)
                S.op("vector", lambda e, pr=pr: e.tensor_reduce(
                    out=m4[:], in_=accb[pr][:], axis=AX.X, op=ALU.add), inc=1)
                S.op("vector", lambda e, pr=pr: e.tensor_tensor(
                    sq4[:], accb[pr][:], accb[pr][:], ALU.mult), inc=1)
                S.op("vector", lambda e: e.tensor_reduce(
                    out=s4[:], in_=sq4[:], axis=AX.X, op=ALU.add), inc=1)
                stat_v = S.val("vector")
                S.op("scalar", lambda e: e.activation(out=mm4[:], in_=m4[:],
                                                      func=AF.Copy, scale=1.0 / D),
                     waits=[("vector", stat_v)], inc=1)
                S.op("scalar", lambda e: e.activation(out=ss4[:], in_=s4[:],
                                                      func=AF.Copy, scale=1.0 / D),
                     inc=1)
                mo_s = S.val("scalar")
                S.op("vector", lambda e: e.tensor_tensor(t4s[:], mm4[:], mm4[:],
                                                         ALU.mult),
                     waits=[("scalar", mo_s)], inc=1)
                JF4()
                S.op("vector", lambda e: e.tensor_tensor(var4[:], ss4[:], t4s[:],
                                                         ALU.subtract), inc=1)
                var_v = S.val("vector")
                S.op("scalar", lambda e: e.activation(out=rstd4[:], in_=var4[:],
                                                      func=AF.Ln, bias=LNEPS),
                     waits=[("vector", var_v)], inc=1)
                S.op("scalar", lambda e: e.activation(out=rstd4[:], in_=rstd4[:],
                                                      func=AF.Exp, scale=-0.5), inc=1)
                rst_s = S.val("scalar")
                S.op("vector", lambda e, pr=pr: e.tensor_scalar(
                    out=outb[pr][:], in0=accb[pr][:], scalar1=mm4[:], scalar2=rstd4[:],
                    op0=ALU.subtract, op1=ALU.mult),
                    waits=[("scalar", rst_s)] +
                          ([out_free[pr]] if out_free[pr][1] else []), inc=1)
                S.op("vector", lambda e, pr=pr: e.tensor_tensor(
                    outb[pr][:], outb[pr][:], x1b[pr][:], ALU.add), inc=1)
                ov = S.val("vector")
                acc_free[pr] = ("vector", ov)
                lt_free[pr] = ("vector", ov)
                S.op("sync", lambda e, tj=tj, pr=pr: e.dma_start(
                    out=out_d[tj * DC:(tj + 1) * DC, :], in_=outb[pr][:]),
                    waits=[("vector", ov)], inc=16)
                out_free[pr] = ("sync", S.val("sync"))
            fin4 = S.val("sync")
            if DBG_STAGE == 9:
                for i in range(10):
                    S.op("sync", lambda e, i=i: e.dma_start(
                        out=out_d[i * DC:(i + 1) * DC, :],
                        in_=x1f_d[(i % 8) * DC:((i % 8) + 1) * DC,
                                  (i // 8) * 1024:(i // 8) * 1024 + 1024]),
                        waits=[("sync", fin4)], inc=16)
                fin4 = S.val("sync")
            if DBG_STAGE == 8:
                last = ("sync", fin4)
                for (src, base) in ((xs_d[0], 0), (zin_d[0], 512)):
                    for i in range(4):
                        S.op("sync", lambda e, src=src, i=i: e.dma_start(
                            out=zg[0][0][:], in_=src[i * DC:(i + 1) * DC, :]),
                            waits=[last], inc=16)
                        zl = S.val("sync")
                        S.op("scalar", lambda e: e.activation(
                            out=ltb[0][:], in_=zg[0][0][:], func=AF.Copy),
                            waits=[("sync", zl)], inc=1)
                        zc = S.val("scalar")
                        S.op("sync", lambda e, base=base, i=i: e.dma_start(
                            out=out_d[base + i * DC: base + (i + 1) * DC, :],
                            in_=ltb[0][:]),
                            waits=[("scalar", zc)], inc=16)
                        last = ("sync", S.val("sync"))
                fin4 = S.val("sync")
            if DBG_STAGE == 6:
                btiles = [zg[0][0], zg[0][1], zg[1][0], zg[1][1]]
                ftiles = [ltb[0], x1b[0], accb[0], outb[0]]
                last = ("sync", fin4)
                cvt = []
                for i in range(4):
                    S.op("sync", lambda e, i=i: e.dma_start(
                        out=btiles[i][:], in_=zr_d[i * DC:(i + 1) * DC, :]),
                        waits=[last], inc=16)
                    zl = S.val("sync")
                    S.op("scalar", lambda e, i=i: e.activation(
                        out=ftiles[i][:], in_=btiles[i][:], func=AF.Copy),
                        waits=[("sync", zl)], inc=1)
                    cvt.append(S.val("scalar"))
                wlast = None
                for i in range(4):
                    S.op("sync", lambda e, i=i: e.dma_start(
                        out=out_d[i * DC:(i + 1) * DC, :], in_=ftiles[i][:]),
                        waits=[("scalar", cvt[i])], inc=16)
                    wlast = ("sync", S.val("sync"))
                S.op("sync", lambda e: e.dma_start(
                    out=btiles[0][:], in_=zr_d[0:DC, :]), waits=[wlast], inc=16)
                zl2 = S.val("sync")
                S.op("scalar", lambda e: e.activation(
                    out=ftiles[0][:], in_=btiles[0][:], func=AF.Copy),
                    waits=[("sync", zl2)], inc=1)
                c2 = S.val("scalar")
                S.op("sync", lambda e: e.dma_start(
                    out=out_d[4 * DC:5 * DC, :], in_=ftiles[0][:]),
                    waits=[("scalar", c2)], inc=16)
                fin4 = S.val("sync")
            S.ops.append(("gpsimd", lambda e: None, [("sync", fin4)], 0))
            st4.close()
            out_written = True


    if not out_written:
        dbg_sb = sb("dbgout", (DC, 4096))
        out_flat = out_d.rearrange("a b -> (a b)")
        if DBG_STAGE >= 1:
            last_g = S.val("gpsimd")
            last_s = S.val("sync")
            for b in range(B):
                S.op("sync", lambda e, b=b: e.dma_start(out=dbg_sb[:], in_=ybin[b]),
                     waits=[("gpsimd", last_g), ("sync", last_s)], inc=16)
                lv = S.val("sync")
                S.op("sync", lambda e, b=b: e.dma_start(
                    out=out_flat[b * DC * 4096:(b + 1) * DC * 4096], in_=dbg_sb[:]),
                    waits=[("sync", lv)], inc=16)
        else:
            S.op("vector", lambda e: e.memset(dbg_sb[:], 0.0), inc=1)
            mv = S.val("vector")
            lg2 = S.val("gpsimd")
            for j in range(4):
                S.op("sync", lambda e, j=j: e.dma_start(
                    out=out_flat[j * DC * 4096:(j + 1) * DC * 4096], in_=dbg_sb[:]),
                    waits=[("vector", mv), ("gpsimd", lg2)], inc=16)
        fin = S.val("sync")
        S.op("gpsimd", lambda e: e.dma_start(out=cnt_i, in_=dbg_sb[0:1, 0:128]),
             waits=[("sync", fin)], inc=16)
        S.ops.append(("gpsimd", lambda e: None, [("gpsimd", S.val("gpsimd"))], 0))

    def _mk(engname):
        def _f(e):
            S.replay(engname, e, sems)
        return _f

    with nc.Block() as block:
        block.sync(_mk("sync"))
        block.vector(_mk("vector"))
        block.scalar(_mk("scalar"))
        block.tensor(_mk("tensor"))
        block.gpsimd(_mk("gpsimd"))

    root.close()
    return nc


def _prep_inputs(ins):
    x = np.ascontiguousarray(np.asarray(ins["L0_attn_x"]).reshape(T, D))
    A_log = np.asarray(ins["L0_attn_A_log"])
    tri = np.zeros((DC, DC), np.float32)
    for k in range(DC):
        tri[k, k + 1:] = 1.0
    onescol = np.ones((DC, 1), np.float32)
    onesrow = np.ones((1, DC), np.float32)
    ident = np.eye(DC, dtype=np.float32)
    iotaE = np.tile(np.arange(8, dtype=np.float32), (DC, 16))
    m9 = iotaE - 9.0
    rwT = np.ascontiguousarray(np.asarray(ins["L0_ffn_router_w"]).T)
    rb = np.ascontiguousarray(np.asarray(ins["L0_ffn_router_b"]).reshape(8, 1))
    wx = np.ascontiguousarray(np.asarray(ins["L0_attn_W_x"]))
    wdt_full = np.asarray(ins["L0_attn_W_dt"])
    bdt_full = np.asarray(ins["L0_attn_b_dt"])
    dsk_full = np.asarray(ins["L0_attn_D"])
    fc1 = np.asarray(ins["L0_ffn_fc1_w"])
    fc2 = np.asarray(ins["L0_ffn_fc2_w"])
    fc1b = np.asarray(ins["L0_ffn_fc1_b"])
    fc2b = np.asarray(ins["L0_ffn_fc2_b"])
    lsc = np.asarray(ins["L0_ffn_layer_scale"])

    base_vsel = np.zeros((DC, NG * DC), np.float32)
    base_rsel = np.zeros((DC, NG * DC), np.float32)
    bselm = np.zeros((N, DC), np.float32)
    for i in range(8):
        for n in range(N):
            bselm[n, i * N + n] = 1.0
    for g in range(NG):
        for i in range(8):
            for n in range(N):
                base_vsel[g * 8 + i, g * DC + i * N + n] = 1.0
                base_rsel[i * N + n, g * DC + g * 8 + i] = 1.0

    maps = []
    for c in range(NCORE):
        dsl = slice(DC * c, DC * (c + 1))
        tok = slice(TC * c, TC * (c + 1))
        aselL = np.full((DC, NG * DC), -1e30, np.float32)
        for g in range(NG):
            for i in range(8):
                aselL[g * 8 + i, g * DC + i * N: g * DC + (i + 1) * N] = \
                    A_log[DC * c + g * 8 + i, :]
        w2cm = np.ascontiguousarray(
            fc2[c].T.reshape(32, DC, 8, DC).transpose(2, 1, 0, 3))
        maps.append({
            "xf": np.ascontiguousarray(x[:, dsl].T),
            "xo": np.ascontiguousarray(x[tok].T),
            "wx": wx,
            "wdt": np.ascontiguousarray(wdt_full[:, dsl]),
            "bdt": np.ascontiguousarray(bdt_full[dsl].reshape(DC, 1)),
            "dsk": np.ascontiguousarray(dsk_full[dsl].reshape(DC, 1)),
            "aselL": aselL, "vsel": base_vsel, "bsel": bselm, "rsel": base_rsel,
            "tri": tri, "onescol": onescol, "onesrow": onesrow, "ident": ident,
            "iotaE": iotaE, "m9": m9,
            "rwT": rwT, "rb": rb,
            "maskC": (np.arange(8) < c).astype(np.float32).reshape(8, 1),
            "w1T": np.ascontiguousarray(fc1[c].T),
            "w1b": np.ascontiguousarray(fc1b[c].reshape(32, DC).T),
            "w2cm": w2cm,
            "w2b": np.ascontiguousarray(fc2b[c].reshape(8, DC).T),
            "lss": np.ascontiguousarray(lsc[c].reshape(8, DC).T),
            "cvec": np.tile(np.array([LNEPS, 1.0, -1.0, -0.5, 1.0 / D, 0.5, 2.0, 0.0],
                                     np.float32), (DC, 1)),
            "yrows": (((c // 2) * 1024 + np.arange(8)[None, :] * DC +
                       np.arange(DC)[:, None]) * 2 + (c % 2)).astype(np.int32),
            "xgrows": (512 * c + np.arange(4)[None, :] * DC +
                       np.arange(DC)[:, None]).astype(np.int32),
        })
    return maps


_cache = {}


def kernel(**inputs):
    in_maps = _prep_inputs(inputs)
    if "nc" not in _cache:
        _cache["nc"] = build_program()
    nc = _cache["nc"]
    res = run_bass_kernel_spmd(nc, in_maps, core_ids=list(range(NCORE)))
    out = np.concatenate([res.results[c]["out"] for c in range(NCORE)], axis=0)
    return out.reshape(B, L, D).astype(np.float32)
